# version 02
# baseline (speedup 1.0000x reference)
import numpy as np
from contextlib import ExitStack

import concourse.bass as bass
import concourse.mybir as mybir
from concourse.bass_utils import run_bass_kernel_spmd

F32 = mybir.dt.float32
BF16 = mybir.dt.bfloat16
AF = mybir.ActivationFunctionType
OP = mybir.AluOpType
AX = mybir.AxisListType

S_ALL, D, H, DK, C = 8192, 2048, 16, 128, 64
EPS = 1e-5
NCORE = 8
TPC = S_ALL // NCORE
NTC = TPC // 128
NCH = TPC // C
AT = 256
NAT = TPC // AT
CPT = AT // C
TCPT = AT // 128
FT = 512
NFT = TPC // FT
E, SH = 16, 2
NE = E + SH
F = 1024
NFC = F // 128
NOC = D // 128
NKC = D // 128
GAMMA = (1.0 - np.exp2(-5.0 - np.arange(H, dtype=np.float64))).astype(np.float32)
EGROUPS = [list(range(i, i + 3)) for i in range(0, 18, 3)]


class ColView:

    def __init__(self, handle, off, width):
        self.h = handle
        self.off = off
        self.w = width

    def __getitem__(self, idx):
        if not isinstance(idx, tuple):
            rows, cols = idx, slice(None)
        else:
            rows, cols = idx
        c0 = cols.start if cols.start is not None else 0
        c1 = cols.stop if cols.stop is not None else self.w
        assert 0 <= c0 <= c1 <= self.w, (c0, c1, self.w)
        return self.h[rows, self.off + c0: self.off + c1]


class Rec:
    ENGINES = ("sync", "gpsimd", "tensor", "vector", "scalar")

    def __init__(self, nc):
        self.nc = nc
        self.steps = {e: [] for e in self.ENGINES}
        self.cnt = {}
        self.sem = {}
        self._last_wait = {}

    def new_sem(self, es, name):
        self.sem[name] = es.enter_context(self.nc.semaphore(name))
        self.cnt[name] = 0

    def wait(self, eng, sname, val=None):
        if val is None:
            val = self.cnt[sname]
        assert val <= self.cnt[sname], (eng, sname, val, self.cnt[sname])
        if val <= 0:
            return
        key = (eng, sname)
        if self._last_wait.get(key, -1) >= val:
            return
        self._last_wait[key] = val
        sem = self.sem[sname]
        self.steps[eng].append(lambda e, sem=sem, val=val: e.wait_ge(sem, val))

    OWN = {"vector": "ve", "scalar": "se", "gpsimd": "gp"}

    def do(self, eng, fn, inc=None, n=1):
        own = self.OWN.get(eng)
        if own is not None:
            self.wait(eng, own)
        if inc is None:
            self.steps[eng].append(fn)
            return None
        sem = self.sem[inc]
        def wrapped(e, fn=fn, sem=sem, n=n):
            ins = fn(e)
            ins.then_inc(sem, n)
        self.steps[eng].append(wrapped)
        self.cnt[inc] += n
        return self.cnt[inc]

    def dma(self, eng, out_ap, in_ap, inc):
        return self.do(eng, lambda e, o=out_ap, i=in_ap: e.dma_start(out=o, in_=i), inc=inc, n=16)

    def emit(self, bk):
        for eng in self.ENGINES:
            steps = self.steps[eng]
            if not steps:
                continue
            getattr(bk, eng)(lambda e, steps=steps: [f(e) for f in steps])


def _host_tables(core):
    t0 = core * TPC
    inv = (1.0 / 10000.0 ** (np.arange(0, DK, 2, dtype=np.float32) / DK)).astype(np.float32)
    pos = np.arange(t0, t0 + TPC, dtype=np.float32)
    ang = pos[:, None] * inv[None, :]
    cosv = np.cos(ang).astype(np.float32)
    sinv = np.sin(ang).astype(np.float32)
    scale = np.float32(DK ** -0.5)
    tab = {}
    tab["ccq"] = np.concatenate([cosv.T, cosv.T], axis=0) * scale
    tab["ssq"] = np.concatenate([sinv.T, sinv.T], axis=0) * scale
    ct = np.zeros((128, NTC * C), np.float32)
    st = np.zeros((128, NTC * C), np.float32)
    for tc in range(NTC):
        ct[:, tc * C:(tc + 1) * C] = cosv[tc * 128:(tc + 1) * 128, :]
        st[:, tc * C:(tc + 1) * C] = sinv[tc * 128:(tc + 1) * 128, :]
    tab["cost"], tab["sint"] = ct, st
    s_idx = (np.arange(128) % C)[:, None].astype(np.float32)
    t_idx = np.arange(C, dtype=np.float32)[None, :]
    rel = t_idx - s_idx
    dh = np.zeros((128, H * C), np.float32)
    cr = np.zeros((128, H * C), np.float32)
    for h in range(H):
        dh[:, h * C:(h + 1) * C] = np.where(rel >= 0, GAMMA[h] ** np.maximum(rel, 0.0), 0.0)
        cr[:, h * C:(h + 1) * C] = GAMMA[h] ** (t_idx + 1.0)
    tab["dhat"], tab["crosst"] = dh, cr
    tab["kdecs"] = (GAMMA[None, :] ** (C - 1.0 - s_idx)).astype(np.float32)
    gc = (GAMMA.astype(np.float64) ** C).astype(np.float32)
    tab["gcs"] = np.broadcast_to(gc[None, :], (128, H)).copy()
    kg = np.zeros((128, NTC * H), np.float32)
    par = (np.arange(128) // C)
    for tc in range(NTC):
        ch = 2 * tc + par
        kg[:, tc * H:(tc + 1) * H] = tab["kdecs"] * (gc[None, :] ** (NCH - 1 - ch)[:, None])
    tab["kdgc"] = kg
    w = np.zeros((NCORE, 128, D), np.float32)
    for bb in range(core):
        for h in range(H):
            w[bb, :, h * DK:(h + 1) * DK] = np.float32(gc[h]) ** np.float32(16 * (core - 1 - bb))
    tab["wbc"] = w
    tab["ident"] = np.eye(128, dtype=np.float32)
    return tab


def _relay_col(Wmat):
    dout = Wmat.shape[1]
    return np.ascontiguousarray(
        Wmat.reshape(16, 128, dout // 128, 128).transpose(2, 1, 0, 3))


def _host_layouts(inputs):
    L = {}
    L["wq_r"] = _relay_col(inputs["L0_attn_Wq"])
    L["wg_r"] = _relay_col(inputs["L0_attn_Wg"])
    L["wo_r"] = _relay_col(inputs["L0_attn_Wo"])
    L["wk"] = inputs["L0_attn_Wk"]
    L["wv"] = inputs["L0_attn_Wv"]
    L["gnw"] = np.ascontiguousarray(inputs["L0_attn_g_norm_w"].reshape(128, 1))
    L["wa_r"] = np.ascontiguousarray(
        inputs["L0_ffn_wa"].reshape(16, 128, E).transpose(1, 0, 2).reshape(128, 16 * E))
    eg = np.concatenate([inputs["L0_ffn_rg"], inputs["L0_ffn_sg"]], axis=0)
    ew1 = np.concatenate([inputs["L0_ffn_rw1"], inputs["L0_ffn_sw1"]], axis=0)
    ew2 = np.concatenate([inputs["L0_ffn_rw2"], inputs["L0_ffn_sw2"]], axis=0)
    L["eg_r"] = np.ascontiguousarray(eg.reshape(NE, 16, 128, NFC, 128).transpose(0, 3, 2, 1, 4))
    L["ew1_r"] = np.ascontiguousarray(ew1.reshape(NE, 16, 128, NFC, 128).transpose(0, 3, 2, 1, 4))
    L["ew2_r"] = np.ascontiguousarray(ew2.reshape(NE, NFC, 128, 16, 128).transpose(0, 3, 2, 1, 4))
    gb = np.concatenate([inputs["L0_ffn_rgb"], inputs["L0_ffn_sgb"]], axis=0)
    b1 = np.concatenate([inputs["L0_ffn_rb1"], inputs["L0_ffn_sb1"]], axis=0)
    L["gb_t"] = np.ascontiguousarray(gb.reshape(NE, NFC, 128).transpose(2, 0, 1).reshape(128, NE * NFC))
    L["b1_t"] = np.ascontiguousarray(b1.reshape(NE, NFC, 128).transpose(2, 0, 1).reshape(128, NE * NFC))
    eo = np.zeros((16, E * 128), np.float32)
    for e in range(E):
        eo[e, e * 128:(e + 1) * 128] = 1.0
    L["eonehot"] = eo
    L["rb2"] = inputs["L0_ffn_rb2"]
    L["sb2"] = inputs["L0_ffn_sb2"]
    return L


def build_nc(mode):
    nc = bass.Bass("TRN2", target_bir_lowering=False)
    es = ExitStack()
    R = Rec(nc)
    for s in ["lc", "lsi", "lx", "la", "gld", "st", "te", "ve", "se", "gp", "wf", "ff", "w2f",
              "wb0", "wb1", "gw0", "gw1", "w20", "w21"]:
        R.new_sem(es, s)

    din = {}
    din["x"] = nc.dram_tensor("d_x", [TPC, D], F32, kind="ExternalInput").ap()
    shapes = {"ccq": [128, TPC], "ssq": [128, TPC], "cost": [128, NTC * C],
              "sint": [128, NTC * C], "dhat": [128, H * C], "crosst": [128, H * C],
              "kdecs": [128, H], "gcs": [128, H], "kdgc": [128, NTC * H],
              "ident": [128, 128], "gnw": [128, 1], "wa_r": [128, 16 * E],
              "wq_r": [16, 128, 16, 128], "wg_r": [16, 128, 16, 128],
              "wo_r": [16, 128, 16, 128], "wk": [D, D], "wv": [D, D],
              "eg_r": [NE, NFC, 128, 16, 128], "ew1_r": [NE, NFC, 128, 16, 128],
              "ew2_r": [NE, 16, 128, NFC, 128],
              "gb_t": [128, NE * NFC], "b1_t": [128, NE * NFC],
              "eonehot": [16, E * 128],
              "rb2": [E, D], "sb2": [SH, D]}
    for nm, shp in shapes.items():
        din[nm] = nc.dram_tensor("d_" + nm, shp, F32, kind="ExternalInput").ap()
    if mode == "summary":
        dout_T = nc.dram_tensor("T_out", [128, D], F32, kind="ExternalOutput").ap()
    else:
        din["T_all"] = nc.dram_tensor("d_T_all", [NCORE, 128, D], F32, kind="ExternalInput").ap()
        din["wbc"] = nc.dram_tensor("d_wbc", [NCORE, 128, D], F32, kind="ExternalInput").ap()
        dout = nc.dram_tensor("out", [TPC, D], F32, kind="ExternalOutput").ap()
        acc_dram = nc.dram_tensor("acc_scratch", [NOC, 128, TPC], F32, kind="Internal").ap()

    b = {}
    def sb(name, shape, dt):
        b[name] = es.enter_context(nc.sbuf_tensor(name, shape, dt))
        return b[name]

    sb("xn_f", [128, NKC * TPC], BF16)
    sb("S", [128, H * DK], F32)
    sb("Sb", [128, H * DK], BF16)
    for nm in ["ccq", "ssq"]:
        sb(nm, [128, TPC], F32)
    for nm in ["cost", "sint"]:
        sb(nm, [128, NTC * C], F32)
    for nm in ["dhat", "crosst"]:
        sb(nm, [128, H * C], F32)
    sb("kdecs", [128, H], F32)
    sb("gcs", [128, H], F32)
    sb("kdgc", [128, NTC * H], F32)
    sb("gnw", [128, 1], F32)
    sb("identb", [128, 128], BF16)
    sb("identf", [128, 128], F32)
    sb("wa_b", [128, 16 * E], BF16)
    sb("onesc", [128, 1], BF16)
    sb("ones1", [1, 128], BF16)
    sb("ones2", [2, FT], BF16)
    sb("gb_b", [128, NE * NFC], F32)
    sb("b1_b", [128, NE * NFC], F32)
    sb("rb2_b", [16, D], BF16)
    sb("eoh_b", [16, E * 128], BF16)
    sb("sb2_b", [2, D], BF16)
    sb("xbuf", [128, D], F32)
    sb("stat", [128, 16], F32)
    sb("xnchunk", [128, D], BF16)
    sb("sdev", [1, H * C], F32)
    sb("rstdf", [1, H * C], F32)
    sb("rstdb", [1, H * C], BF16)
    sb("silug", [128, H * C], F32)
    sb("tmp1", [128, H * C], F32)
    sb("ropeA", [128, D], F32)
    sb("ropeU", [128, D], F32)
    sb("uni", [128, 34816], BF16)
    _off = [0]
    def uv(name, width):
        b[name] = ColView(b["uni"], _off[0], width)
        _off[0] += width
    for _nm, _w in [("qh", H * AT), ("gh", H * AT), ("ktok", TCPT * D),
                    ("vtok", TCPT * D), ("ogat", H * AT), ("wbuf", 2 * NKC * 128),
                    ("qt", H * C), ("adm", H * C), ("kfs", H * C),
                    ("osq", H * C), ("kts", D)]:
        uv(_nm, _w)
    if mode == "main":
        sb("combine", [16, TPC], F32)
        sb("combb", [16, TPC], BF16)
        sb("zt", [128, 16], F32)
        sb("eq1", [128, 16], F32)
        sb("eq2", [128, 16], F32)
        sb("zmask", [128, 16], F32)
        sb("p12", [128, 4], F32)
        sb("acct", [128, 2 * FT], F32)
        b["ytil"] = ColView(b["uni"], 0, 3 * NFC * TPC)
        b["gwbuf"] = ColView(b["uni"], 3 * NFC * TPC, 2 * NKC * 128)
        b["w1buf"] = ColView(b["uni"], 3 * NFC * TPC + 2 * NKC * 128, 2 * NKC * 128)
        b["w2buf"] = ColView(b["uni"], 3 * NFC * TPC + 4 * NKC * 128, 2 * NFC * 128)
    else:
        sb("Tout", [128, D], F32)

    PSA = es.enter_context(nc.psum_tensor("PSA", [128, 2048], F32))
    PSB = es.enter_context(nc.psum_tensor("PSB", [128, 2048], F32))

    def bankA(i, w=512):
        return PSA[:, i * 512: i * 512 + w]

    def bankB(i, w=512):
        return PSB[:, i * 512: i * 512 + w]

    cl = ["ccq", "ssq", "cost", "sint", "dhat", "crosst", "kdecs", "gcs", "kdgc", "gnw"]
    for nm in cl:
        R.dma("sync", b[nm][:], din[nm], "lc")
    R.dma("sync", b["identf"][:], din["ident"], "lc")
    R.dma("sync", b["gb_b"][:], din["gb_t"], "lc")
    R.dma("sync", b["b1_b"][:], din["b1_t"], "lc")
    R.dma("gpsimd", b["identb"][:], din["ident"], "gld")
    R.dma("gpsimd", b["wa_b"][:], din["wa_r"], "gld")
    R.dma("gpsimd", b["rb2_b"][:], din["rb2"], "gld")
    R.dma("gpsimd", b["eoh_b"][:], din["eonehot"], "gld")
    R.dma("gpsimd", b["sb2_b"][:], din["sb2"], "gld")
    R.do("gpsimd", lambda g: g.memset(b["onesc"][:], 1.0), inc="gp")
    R.do("gpsimd", lambda g: g.memset(b["ones1"][:], 1.0), inc="gp")
    R.do("gpsimd", lambda g: g.memset(b["ones2"][:], 1.0), inc="gp")
    R.do("gpsimd", lambda g: g.memset(b["stat"][:, 6:7], float(EPS)), inc="gp")
    R.do("gpsimd", lambda g: g.memset(b["sdev"][:, 0:1], float(EPS)), inc="gp")
    consts_ld, consts_gld, consts_gp = R.cnt["lc"], R.cnt["gld"], R.cnt["gp"]

    if mode == "main":
        b["Tbuf"] = b["ropeA"]
        b["wbcbuf"] = b["ropeU"]
        b["Stmp"] = b["xbuf"]
        b["ffs"] = b["tmp1"]
        b["outb"] = b["ropeU"]
        b["accl"] = b["ropeA"]
        for bb in range(NCORE):
            R.wait("sync", "ve")
            R.dma("sync", b["Tbuf"][:], din["T_all"][bb], "lsi")
            R.dma("sync", b["wbcbuf"][:], din["wbc"][bb], "lsi")
            R.wait("vector", "lsi")
            R.do("vector", lambda v: v.tensor_mul(out=b["Stmp"][:, 0:D], in0=b["Tbuf"][:], in1=b["wbcbuf"][:]), inc="ve")
            if bb == 0:
                R.do("vector", lambda v: v.tensor_copy(out=b["S"][:], in_=b["Stmp"][:, 0:D]), inc="ve")
            else:
                R.do("vector", lambda v: v.tensor_add(out=b["S"][:], in0=b["S"][:], in1=b["Stmp"][:, 0:D]), inc="ve")
        R.wait("scalar", "ve")
        R.do("scalar", lambda s: s.copy(out=b["Sb"][:], in_=b["S"][:]), inc="se")

    tpv = [PSA[:, 0:64].bitcast(BF16), PSA[:, 512:576].bitcast(BF16)]
    for _eng in ("scalar", "vector"):
        R.wait(_eng, "gp", consts_gp)
        R.wait(_eng, "lc", consts_ld)
        R.wait(_eng, "gld", consts_gld)
    xn_done = []
    for j in range(NTC):
        xb = b["xbuf"][:, 0:D]
        if j >= 1:
            R.wait("sync", "ve", xn_done[j - 1])
        R.wait("sync", "ve") if j == 0 else None
        R.wait("sync", "se")
        R.dma("sync", xb, din["x"][j * 128:(j + 1) * 128, :], "lx")
        R.wait("vector", "lx")
        R.do("vector", lambda v, xb=xb: v.reduce_sum(out=b["stat"][:, 0:1], in_=xb, axis=AX.X), inc="ve")
        R.wait("scalar", "lx")
        R.do("scalar", lambda s, xb=xb: s.activation(out=b["ropeU"][:], in_=xb, func=AF.Square), inc="se")
        R.wait("vector", "se")
        R.do("vector", lambda v: v.reduce_sum(out=b["stat"][:, 1:2], in_=b["ropeU"][:], axis=AX.X), inc="ve")
        R.do("vector", lambda v: v.tensor_scalar_mul(out=b["stat"][:, 2:3], in0=b["stat"][:, 0:1], scalar1=1.0 / D), inc="ve")
        R.do("vector", lambda v: v.scalar_tensor_tensor(out=b["stat"][:, 3:4], in0=b["stat"][:, 2:3], scalar=-1.0, in1=b["stat"][:, 2:3], op0=OP.mult, op1=OP.mult), inc="ve")
        R.do("vector", lambda v: v.scalar_tensor_tensor(out=b["stat"][:, 3:4], in0=b["stat"][:, 1:2], scalar=1.0 / D, in1=b["stat"][:, 3:4], op0=OP.mult, op1=OP.add), inc="ve")
        R.wait("scalar", "ve")
        R.do("scalar", lambda s: s.activation(out=b["stat"][:, 4:5], in_=b["stat"][:, 3:4], func=AF.Sqrt, bias=b["stat"][:, 6:7], scale=1.0), inc="se")
        R.wait("vector", "se")
        R.do("vector", lambda v: v.reciprocal(out=b["stat"][:, 5:6], in_=b["stat"][:, 4:5]), inc="ve")
        xn_ve = R.do("vector", lambda v, xb=xb: v.tensor_scalar(out=b["xnchunk"][:], in0=xb, scalar1=b["stat"][:, 2:3], scalar2=b["stat"][:, 5:6], op0=OP.subtract, op1=OP.mult), inc="ve")
        xn_done.append(xn_ve)
        if j == 0:
            R.wait("tensor", "lc", consts_ld)
            R.wait("tensor", "gld", consts_gld)
            R.wait("tensor", "gp", consts_gp)
        R.wait("tensor", "ve", xn_ve)
        for kc in range(NKC):
            src = b["xnchunk"][:, kc * 128:(kc + 1) * 128]
            dst = tpv[kc % 2]
            R.wait("tensor", "ve")
            R.do("tensor", lambda t, d=dst, s=src: t.transpose(d, s, b["identb"][:]), inc="te")
            R.wait("vector", "te")
            out = b["xn_f"][:, kc * TPC + j * 128: kc * TPC + (j + 1) * 128]
            ev = R.do("vector", lambda v, o=out, d=dst: v.tensor_copy(out=o, in_=d), inc="ve")
            if kc + 1 < NKC:
                pass
        R.wait("tensor", "ve")

    abv = PSA[:, 0:1024]
    obv = PSA[:, 1024:2048]
    ubv = PSB[:, 0:1024]
    ktrv = PSB[:, 1024:1536].bitcast(BF16)
    uaccv = PSB[:]

    wslot_ap = [b["wbuf"][:, 0:NKC * 128], b["wbuf"][:, NKC * 128:2 * NKC * 128]]

    def rope_fmajor(ppq, h, t00, dst_buf):
        cc = b["ccq"][:, t00:t00 + AT]
        ss = b["ssq"][:, t00:t00 + AT]
        R.wait("scalar", "te")
        R.do("scalar", lambda s, p=ppq: s.mul(out=b["ropeA"][0:64, 0:AT], in_=p[64:128, :], mul=-1.0), inc="se")
        R.do("scalar", lambda s, p=ppq: s.copy(out=b["ropeA"][64:128, 0:AT], in_=p[0:64, :]), inc="se")
        R.wait("vector", "te")
        R.wait("vector", "se")
        R.do("vector", lambda v, p=ppq, cc=cc: v.tensor_mul(out=b["ropeU"][:, 0:AT], in0=p, in1=cc), inc="ve")
        R.do("vector", lambda v, ss=ss: v.tensor_mul(out=b["ropeA"][:, 0:AT], in0=b["ropeA"][:, 0:AT], in1=ss), inc="ve")
        dst = dst_buf[:, h * AT:(h + 1) * AT]
        return R.do("vector", lambda v, d=dst: v.tensor_add(out=d, in0=b["ropeU"][:, 0:AT], in1=b["ropeA"][:, 0:AT]), inc="ve")

    def rope_tmajor(ppk, tc_global, dst_ap, decay_tab=None):
        ct = b["cost"][:, tc_global * C:(tc_global + 1) * C]
        st = b["sint"][:, tc_global * C:(tc_global + 1) * C]
        ct4 = ct.unsqueeze(1).unsqueeze(1).broadcast_to([128, H, 2, C])
        st4 = st.unsqueeze(1).unsqueeze(1).broadcast_to([128, H, 2, C])
        lo = [(128, H), (1, C)]
        def vw(ap, off):
            return ap.rearrange("p (h x) -> p h x", h=H)[:, :, off:off + C]
        R.wait("scalar", "te")
        R.do("scalar", lambda s, p=ppk: s.mul(out=vw(b["ropeA"][:], 0), in_=vw(p, C), mul=-1.0), inc="se")
        R.do("scalar", lambda s, p=ppk: s.copy(out=vw(b["ropeA"][:], C), in_=vw(p, 0)), inc="se")
        R.wait("vector", "te")
        R.wait("vector", "se")
        pp4 = ppk.rearrange("p (h x c) -> p h x c", h=H, x=2)
        R.do("vector", lambda v, p=pp4, c=ct4: v.tensor_tensor(out=b["ropeU"][:].rearrange("p (h x c) -> p h x c", h=H, x=2), in0=p, in1=c, op=OP.mult), inc="ve")
        ra4 = b["ropeA"][:].rearrange("p (h x c) -> p h x c", h=H, x=2)
        R.do("vector", lambda v, r=ra4, s4=st4: v.tensor_tensor(out=r, in0=r, in1=s4, op=OP.mult), inc="ve")
        if decay_tab is None:
            return R.do("vector", lambda v, d=dst_ap: v.tensor_add(out=d, in0=b["ropeU"][:], in1=b["ropeA"][:]), inc="ve")
        else:
            R.do("vector", lambda v: v.tensor_add(out=b["ropeU"][:], in0=b["ropeU"][:], in1=b["ropeA"][:]), inc="ve")
            dt3 = decay_tab.unsqueeze(2).broadcast_to([128, H, DK])
            ru3 = b["ropeU"][:].rearrange("p (h d) -> p h d", h=H)
            return R.do("vector", lambda v, d=dst_ap, r=ru3, t3=dt3: v.tensor_tensor(out=d.rearrange("p (h d) -> p h d", h=H), in0=r, in1=t3, op=OP.mult), inc="ve")

    wk_idx = [0]
    wb_uses = []

    def stream_colblock(src_ap):
        k = wk_idx[0]
        slot = k % 2
        if k >= 2:
            R.wait("gpsimd", "te", wb_uses[k - 2])
        R.dma("gpsimd", wslot_ap[slot], src_ap, "wb%d" % slot)
        wk_idx[0] += 1
        return slot, R.cnt["wb%d" % slot]

    for a in range(NAT):
        t00 = a * AT
        if mode == "main":
            for wi, (wnm, dstb) in enumerate([("wq_r", "qh"), ("wg_r", "gh")]):
                for oc in range(H):
                    slot, wb_need = stream_colblock(din[wnm][oc].rearrange("p kc m -> p (kc m)"))
                    pq = bankA(2 * (oc % 2), AT)
                    R.wait("tensor", "wb%d" % slot, wb_need)
                    R.wait("tensor", "ve")
                    R.wait("tensor", "se")
                    def mmq(t, pq=pq, slot=slot, t00=t00):
                        for kc in range(NKC):
                            mm = t.matmul(pq, wslot_ap[slot][:, kc * 128:(kc + 1) * 128],
                                     b["xn_f"][:, kc * TPC + t00: kc * TPC + t00 + AT],
                                     start=(kc == 0), stop=(kc == NKC - 1))
                        return mm
                    R.do("tensor", mmq, inc="te")
                    wb_uses.append(R.cnt["te"])
                    if wnm == "wq_r":
                        rope_fmajor(pq, oc, t00, b["qh"])
                    else:
                        R.wait("vector", "te")
                        dst = b["gh"][:, oc * AT:(oc + 1) * AT]
                        R.do("vector", lambda v, d=dst, p=pq: v.tensor_copy(out=d, in_=p), inc="ve")

        for tc in range(TCPT):
            tcg = a * TCPT + tc
            for wnm in ["wv", "wk"]:
                row_gld = []
                for kc in range(NKC):
                    k = wk_idx[0]
                    slot = k % 2
                    if k >= 2:
                        R.wait("gpsimd", "te", wb_uses[k - 2])
                    R.dma("gpsimd", wslot_ap[slot], din[wnm][kc * 128:(kc + 1) * 128, :], "wb%d" % slot)
                    wk_idx[0] += 1
                    psum_t = PSA if (mode == "summary" or wnm == "wv") else PSB
                    lhs = b["xn_f"][:, kc * TPC + t00 + tc * 128: kc * TPC + t00 + (tc + 1) * 128]
                    R.wait("tensor", "wb%d" % slot, R.cnt["wb%d" % slot])
                    if kc == 0:
                        R.wait("tensor", "ve")
                        R.wait("tensor", "se")
                    def mmvk(t, lhs=lhs, slot=slot, kc=kc, psum_t=psum_t):
                        for fs in range(4):
                            mm = t.matmul(psum_t[:, fs * 512:(fs + 1) * 512], lhs,
                                     wslot_ap[slot][:, fs * 512:(fs + 1) * 512],
                                     start=(kc == 0), stop=(kc == NKC - 1))
                        return mm
                    R.do("tensor", mmvk, inc="te")
                    wb_uses.append(R.cnt["te"])
                if wnm == "wv":
                    R.wait("vector", "te")
                    dst = b["vtok"][:, tc * D:(tc + 1) * D]
                    psum_t = PSA if (mode == "summary") else PSA
                    R.do("vector", lambda v, d=dst, p=psum_t: v.tensor_copy(out=d, in_=p[:]), inc="ve")
                else:
                    psum_t = PSA if mode == "summary" else PSB
                    dst = b["ktok"][:, tc * D:(tc + 1) * D]
                    if mode == "summary":
                        rope_tmajor(psum_t[:], tcg, dst, decay_tab=b["kdgc"][:, tcg * H:(tcg + 1) * H])
                    else:
                        rope_tmajor(psum_t[:], tcg, dst)

            if mode == "summary":
                for par in range(2):
                    ch = 2 * tcg + par
                    p0 = par * 64
                    R.wait("tensor", "ve")
                    for h in range(H):
                        lhs = b["ktok"][p0:p0 + 64, tc * D + h * DK: tc * D + (h + 1) * DK]
                        rhs = b["vtok"][p0:p0 + 64, tc * D + h * DK: tc * D + (h + 1) * DK]
                        dst = uaccv[:, h * DK:(h + 1) * DK]
                        R.do("tensor", lambda t, d=dst, l=lhs, r=rhs, ch=ch, h=h: t.matmul(
                            d, l, r, start=(ch == 0 and h % 4 == 0),
                            stop=(ch == NCH - 1 and h % 4 == 3)),
                            inc="te" if h == H - 1 else None)

        if mode == "main":
            for n in range(CPT):
                tc = n // 2
                par = n % 2
                p0 = par * 64
                chs = n * C
                R.wait("vector", "te")
                kd3 = b["kdecs"][p0:p0 + 64, :].unsqueeze(2).broadcast_to([64, H, DK])
                src3 = b["ktok"][p0:p0 + 64, tc * D:(tc + 1) * D].rearrange("p (h d) -> p h d", h=H)
                dst3 = b["kts"][p0:p0 + 64, :].rearrange("p (h d) -> p h d", h=H)
                kts_ve = R.do("vector", lambda v, d=dst3, s=src3, k3=kd3: v.tensor_tensor(out=d, in0=s, in1=k3, op=OP.mult), inc="ve")
                qv = b["qh"][:].rearrange("p (h t) -> p h t", h=H)[:, :, chs:chs + C]
                qt_ve = R.do("vector", lambda v, qv=qv: v.tensor_tensor(
                    out=b["qt"][:].rearrange("p (h c) -> p h c", h=H),
                    in0=qv, in1=b["crosst"][:].rearrange("p (h c) -> p h c", h=H), op=OP.mult), inc="ve")
                R.wait("tensor", "ve", kts_ve)
                for h in range(H):
                    src = b["ktok"][p0:p0 + 64, tc * D + h * DK: tc * D + (h + 1) * DK]
                    dst = ktrv[:, h * C:(h + 1) * C]
                    idn = b["identb"][p0:p0 + 64, p0:p0 + 64]
                    R.do("tensor", lambda t, d=dst, s=src, i=idn: t.transpose(d, s, i), inc="te" if h == H - 1 else None)
                R.wait("vector", "te")
                kfs_ve = R.do("vector", lambda v: v.tensor_copy(out=b["kfs"][:], in_=ktrv), inc="ve")
                R.wait("tensor", "ve", kfs_ve)
                def mm_ah(t, p0=p0, chs=chs):
                    for h in range(H):
                        mm = t.matmul(abv[p0:p0 + 64, h * C:(h + 1) * C] if p0 == 0 else abv[64:128, h * C:(h + 1) * C],
                                 b["kfs"][:, h * C:(h + 1) * C],
                                 b["qh"][:, h * AT + chs: h * AT + chs + C],
                                 start=True, stop=True,
                                 tile_position=(0, p0))
                    return mm
                R.do("tensor", mm_ah, inc="te")
                R.wait("vector", "te")
                adm_ve = R.do("vector", lambda v, p0=p0: v.tensor_tensor(
                    out=b["adm"][p0:p0 + 64, :], in0=abv[p0:p0 + 64, :] if p0 == 0 else abv[64:128, :],
                    in1=b["dhat"][p0:p0 + 64, :], op=OP.mult), inc="ve")
                R.wait("tensor", "ve", adm_ve)
                R.wait("tensor", "se")
                def mm_o(t, p0=p0, tc=tc):
                    for h in range(H):
                        mm = t.matmul(obv[:, h * C:(h + 1) * C],
                                 b["vtok"][p0:p0 + 64, tc * D + h * DK: tc * D + (h + 1) * DK],
                                 b["adm"][p0:p0 + 64, h * C:(h + 1) * C],
                                 start=True, stop=False)
                        mm = t.matmul(obv[:, h * C:(h + 1) * C],
                                 b["Sb"][:, h * DK:(h + 1) * DK],
                                 b["qt"][:, h * C:(h + 1) * C],
                                 start=False, stop=True)
                    return mm
                R.do("tensor", mm_o, inc="te")
                o_te = R.cnt["te"]
                for hh in range(2):
                    R.wait("tensor", "ve")
                    def mm_u(t, p0=p0, tc=tc, hh=hh):
                        for h in range(hh * 8, hh * 8 + 8):
                            mm = t.matmul(ubv[:, (h % 8) * DK:((h % 8) + 1) * DK],
                                     b["kts"][p0:p0 + 64, h * DK:(h + 1) * DK],
                                     b["vtok"][p0:p0 + 64, tc * D + h * DK: tc * D + (h + 1) * DK],
                                     start=True, stop=True)
                        return mm
                    R.do("tensor", mm_u, inc="te")
                    R.wait("vector", "te")
                    Sh = b["S"][:, hh * 1024:(hh + 1) * 1024]
                    g3 = b["gcs"][:, hh * 8:(hh + 1) * 8].unsqueeze(2).broadcast_to([128, 8, DK])
                    R.do("vector", lambda v, S=Sh, g3=g3: v.tensor_tensor(
                        out=S.rearrange("p (h d) -> p h d", h=8), in0=S.rearrange("p (h d) -> p h d", h=8), in1=g3, op=OP.mult), inc="ve")
                    upd_ve = R.do("vector", lambda v, S=Sh: v.tensor_add(out=S, in0=S, in1=ubv), inc="ve")
                    R.wait("scalar", "ve", upd_ve)
                    R.do("scalar", lambda s, hh=hh: s.copy(out=b["Sb"][:, hh * 1024:(hh + 1) * 1024], in_=b["S"][:, hh * 1024:(hh + 1) * 1024]), inc="se")
                R.wait("scalar", "te", o_te)
                osq_se = R.do("scalar", lambda s: s.activation(out=b["osq"][:], in_=obv, func=AF.Square), inc="se")
                R.wait("tensor", "se", osq_se)
                R.wait("tensor", "ve", adm_ve)
                def mm_ssum(t):
                    mm = t.matmul(abv[0:1, 0:512], b["onesc"][:], b["osq"][:, 0:512], start=True, stop=True)
                    mm = t.matmul(abv[0:1, 512:1024], b["onesc"][:], b["osq"][:, 512:1024], start=True, stop=True)
                    return mm
                R.do("tensor", mm_ssum, inc="te")
                R.wait("scalar", "te")
                R.do("scalar", lambda s: s.activation(out=b["sdev"][:, 0:512], in_=abv[0:1, 0:512], func=AF.Sqrt, bias=b["sdev"][:, 0:1], scale=1.0 / DK), inc="se")
                R.do("scalar", lambda s: s.activation(out=b["sdev"][:, 512:1024], in_=abv[0:1, 512:1024], func=AF.Sqrt, bias=b["sdev"][:, 0:1], scale=1.0 / DK), inc="se")
                R.wait("vector", "se")
                R.do("vector", lambda v: v.reciprocal(out=b["rstdf"][:], in_=b["sdev"][:]), inc="ve")
                rstd_ve = R.do("vector", lambda v: v.tensor_copy(out=b["rstdb"][:], in_=b["rstdf"][:]), inc="ve")
                R.wait("tensor", "ve", rstd_ve)
                def mm_bc(t):
                    mm = t.matmul(ubv[:, 0:512], b["ones1"][:], b["rstdb"][:, 0:512], start=True, stop=True)
                    mm = t.matmul(ubv[:, 512:1024], b["ones1"][:], b["rstdb"][:, 512:1024], start=True, stop=True)
                    return mm
                R.do("tensor", mm_bc, inc="te")
                bc_te = R.cnt["te"]
                gv = b["gh"][:].rearrange("p (h t) -> p h t", h=H)[:, :, chs:chs + C]
                R.do("scalar", lambda s, gv=gv: s.activation(out=b["silug"][:].rearrange("p (h c) -> p h c", h=H), in_=gv, func=AF.Silu), inc="se")
                R.wait("vector", "se")
                R.do("vector", lambda v: v.scalar_tensor_tensor(out=b["tmp1"][:], in0=obv, scalar=b["gnw"][:], in1=b["silug"][:], op0=OP.mult, op1=OP.mult), inc="ve")
                R.wait("vector", "te", bc_te)
                ov = b["ogat"][:].rearrange("p (h t) -> p h t", h=H)[:, :, chs:chs + C]
                R.do("vector", lambda v, ov=ov: v.tensor_tensor(
                    out=ov, in0=b["tmp1"][:].rearrange("p (h c) -> p h c", h=H),
                    in1=ubv.rearrange("p (h c) -> p h c", h=H), op=OP.mult), inc="ve")

            wo_store_st = []
            for oc in range(NOC):
                slot, wb_need = stream_colblock(din["wo_r"][oc].rearrange("p kc m -> p (kc m)"))
                pq = bankA(2 * (oc % 2), AT)
                R.wait("tensor", "wb%d" % slot, wb_need)
                R.wait("tensor", "ve")
                R.wait("tensor", "se")
                def mm_wo(t, pq=pq, slot=slot):
                    for h in range(H):
                        mm = t.matmul(pq, wslot_ap[slot][:, h * 128:(h + 1) * 128],
                                 b["ogat"][:, h * AT:(h + 1) * AT],
                                 start=(h == 0), stop=(h == H - 1))
                    return mm
                R.do("tensor", mm_wo, inc="te")
                wb_uses.append(R.cnt["te"])
                R.wait("vector", "te")
                dst = b["acct"][:, (oc % 4) * AT:(oc % 4) * AT + AT]
                R.wait("vector", "st")
                xnsl = b["xn_f"][:, oc * TPC + t00: oc * TPC + t00 + AT]
                se_tok = R.do("vector", lambda v, d=dst, p=pq, xs=xnsl: v.tensor_add(out=d, in0=p, in1=xs), inc="ve")
                R.wait("sync", "ve", se_tok)
                R.dma("sync", acc_dram[oc][:, t00:t00 + AT], dst, "st")
                wo_store_st.append(R.cnt["st"])

    if mode == "summary":
        R.wait("vector", "te")
        R.do("vector", lambda v: v.tensor_copy(out=b["Tout"][:], in_=uaccv), inc="ve")
        R.wait("sync", "ve")
        R.dma("sync", dout_T, b["Tout"][:], "st")
        R.wait("sync", "st")
        with nc.Block() as bk:
            R.emit(bk)
        es.close()
        return nc

    affv = PSB[:, 0:16]
    ctrv = PSB[0:16, 512:640]
    for j in range(NTC):
        R.wait("tensor", "ve")
        def mm_aff(t, j=j):
            for kc in range(NKC):
                mm = t.matmul(affv, b["xn_f"][:, kc * TPC + j * 128: kc * TPC + (j + 1) * 128],
                         b["wa_b"][:, kc * 16:(kc + 1) * 16],
                         start=(kc == 0), stop=(kc == NKC - 1))
            return mm
        R.do("tensor", mm_aff, inc="te")
        R.wait("vector", "te")
        R.do("vector", lambda v: v.tensor_copy(out=b["zt"][:], in_=affv), inc="ve")
        R.do("vector", lambda v: v.reduce_max(out=b["stat"][:, 8:9], in_=b["zt"][:], axis=AX.X), inc="ve")
        R.do("vector", lambda v: v.tensor_scalar(out=b["eq1"][:], in0=b["zt"][:], scalar1=b["stat"][:, 8:9], scalar2=None, op0=OP.is_equal), inc="ve")
        R.do("vector", lambda v: v.scalar_tensor_tensor(out=b["zmask"][:], in0=b["eq1"][:], scalar=-1e30, in1=b["zt"][:], op0=OP.mult, op1=OP.add), inc="ve")
        R.do("vector", lambda v: v.reduce_max(out=b["stat"][:, 9:10], in_=b["zmask"][:], axis=AX.X), inc="ve")
        R.do("vector", lambda v: v.tensor_scalar(out=b["eq2"][:], in0=b["zmask"][:], scalar1=b["stat"][:, 9:10], scalar2=None, op0=OP.is_equal), inc="ve")
        sig_ve = R.do("vector", lambda v: v.tensor_copy(out=b["p12"][:, 0:2], in_=b["stat"][:, 8:10]), inc="ve")
        R.wait("scalar", "ve", sig_ve)
        R.do("scalar", lambda s: s.activation(out=b["p12"][:, 2:4], in_=b["p12"][:, 0:2], func=AF.Sigmoid), inc="se")
        R.wait("vector", "se")
        R.do("vector", lambda v: v.tensor_add(out=b["stat"][:, 10:11], in0=b["p12"][:, 2:3], in1=b["p12"][:, 3:4]), inc="ve")
        R.do("vector", lambda v: v.reciprocal(out=b["stat"][:, 11:12], in_=b["stat"][:, 10:11]), inc="ve")
        R.do("vector", lambda v: v.tensor_mul(out=b["p12"][:, 0:1], in0=b["p12"][:, 2:3], in1=b["stat"][:, 11:12]), inc="ve")
        R.do("vector", lambda v: v.tensor_mul(out=b["p12"][:, 1:2], in0=b["p12"][:, 3:4], in1=b["stat"][:, 11:12]), inc="ve")
        R.do("vector", lambda v: v.tensor_scalar(out=b["zt"][:], in0=b["eq1"][:], scalar1=b["p12"][:, 0:1], scalar2=None, op0=OP.mult), inc="ve")
        ct_ve = R.do("vector", lambda v: v.scalar_tensor_tensor(out=b["zt"][:], in0=b["eq2"][:], scalar=b["p12"][:, 1:2], in1=b["zt"][:], op0=OP.mult, op1=OP.add), inc="ve")
        R.wait("tensor", "ve", ct_ve)
        R.do("tensor", lambda t: t.transpose(ctrv, b["zt"][:], b["identf"][:]), inc="te")
        R.wait("vector", "te")
        dst = b["combine"][:, j * 128:(j + 1) * 128]
        R.do("vector", lambda v, d=dst: v.tensor_copy(out=d, in_=ctrv), inc="ve")
    R.do("vector", lambda v: v.tensor_copy(out=b["combb"][:], in_=b["combine"][:]), inc="ve")

    h1v = [bankA(0), bankA(1)]
    h2v = [bankA(2), bankA(3)]
    bcv = [bankB(0), bankB(1)]
    w2v = [bankB(2), bankB(3)]
    gw_ap = [b["gwbuf"][:, 0:NKC * 128], b["gwbuf"][:, NKC * 128:]]
    w1_ap = [b["w1buf"][:, 0:NKC * 128], b["w1buf"][:, NKC * 128:]]
    w2_ap = [b["w2buf"][:, 0:NFC * 128], b["w2buf"][:, NFC * 128:]]
    mcount = [0]
    ff_uses = []
    w2count = [0]
    w2_uses = []
    acc_store_cnt = {}

    for gi, group in enumerate(EGROUPS):
        for ei, e in enumerate(group):
            if e < E:
                R.wait("tensor", "ve")
                def mm_bcc(t, e=e):
                    lhs = b["eoh_b"][0:16, e * 128:(e + 1) * 128]
                    mm = t.matmul(bcv[0], lhs, b["combb"][0:16, 0:512], start=True, stop=True)
                    mm = t.matmul(bcv[1], lhs, b["combb"][0:16, 512:1024], start=True, stop=True)
                    return mm
                R.do("tensor", mm_bcc, inc="te")
            for fc in range(NFC):
                m = mcount[0]
                slot = m % 2
                if m >= 2:
                    R.wait("gpsimd", "te", ff_uses[m - 2])
                R.dma("gpsimd", gw_ap[slot], din["eg_r"][e, fc].rearrange("p kc m -> p (kc m)"), "gw%d" % slot)
                R.dma("gpsimd", w1_ap[slot], din["ew1_r"][e, fc].rearrange("p kc m -> p (kc m)"), "gw%d" % slot)
                mcount[0] += 1
                R.wait("tensor", "gw%d" % slot, R.cnt["gw%d" % slot])
                R.wait("tensor", "se")
                R.wait("tensor", "ve")
                def mm_h12(t, slot=slot):
                    for kc in range(NKC):
                        for tt in range(NFT):
                            rhs = b["xn_f"][:, kc * TPC + tt * FT: kc * TPC + (tt + 1) * FT]
                            mm = t.matmul(h1v[tt], gw_ap[slot][:, kc * 128:(kc + 1) * 128], rhs,
                                     start=(kc == 0), stop=(kc == NKC - 1))
                            mm = t.matmul(h2v[tt], w1_ap[slot][:, kc * 128:(kc + 1) * 128], rhs,
                                     start=(kc == 0), stop=(kc == NKC - 1))
                    return mm
                R.do("tensor", mm_h12, inc="te")
                ff_uses.append(R.cnt["te"])
                bcol = b["gb_b"][:, e * NFC + fc: e * NFC + fc + 1]
                b1col = b["b1_b"][:, e * NFC + fc: e * NFC + fc + 1]
                R.wait("scalar", "te")
                for tt in range(NFT):
                    R.do("scalar", lambda s, tt=tt, bc=bcol: s.activation(
                        out=b["ropeU"][:, tt * FT:(tt + 1) * FT], in_=h1v[tt], func=AF.Gelu, bias=bc, scale=1.0), inc="se")
                R.wait("vector", "se")
                R.wait("vector", "te")
                for tt in range(NFT):
                    R.do("vector", lambda v, tt=tt, b1c=b1col: v.scalar_tensor_tensor(
                        out=b["ffs"][:, tt * FT:(tt + 1) * FT], in0=h2v[tt], scalar=b1c,
                        in1=b["ropeU"][:, tt * FT:(tt + 1) * FT], op0=OP.add, op1=OP.mult), inc="ve")
                    ydst = b["ytil"][:, ei * (NFC * TPC) + fc * TPC + tt * FT: ei * (NFC * TPC) + fc * TPC + (tt + 1) * FT]
                    if e < E:
                        R.do("vector", lambda v, d=ydst, tt=tt: v.tensor_mul(
                            out=d, in0=b["ffs"][:, tt * FT:(tt + 1) * FT], in1=bcv[tt]), inc="ve")
                    else:
                        R.do("vector", lambda v, d=ydst, tt=tt: v.tensor_copy(
                            out=d, in_=b["ffs"][:, tt * FT:(tt + 1) * FT]), inc="ve")
        for oc in range(NOC):
            for ei, e in enumerate(group):
                m2 = w2count[0]
                slot = m2 % 2
                if m2 >= 2:
                    R.wait("gpsimd", "te", w2_uses[m2 - 2])
                R.dma("gpsimd", w2_ap[slot], din["ew2_r"][e, oc].rearrange("p fc m -> p (fc m)"), "w2%d" % slot)
                w2count[0] += 1
                R.wait("tensor", "w2%d" % slot, R.cnt["w2%d" % slot])
                if ei == 0:
                    R.wait("tensor", "ve")
                def mm_w2(t, slot=slot, ei=ei, e=e, gi=gi, oc=oc):
                    first = (ei == 0)
                    last = (ei == len(EGROUPS[0]) - 1)
                    for tt in range(NFT):
                        for fc in range(NFC):
                            mm = t.matmul(w2v[tt], w2_ap[slot][:, fc * 128:(fc + 1) * 128],
                                     b["ytil"][:, ei * (NFC * TPC) + fc * TPC + tt * FT: ei * (NFC * TPC) + fc * TPC + (tt + 1) * FT],
                                     start=(first and fc == 0), stop=False)
                        if gi == 0 and last:
                            mm = t.matmul(w2v[tt], b["rb2_b"][0:16, oc * 128:(oc + 1) * 128],
                                     b["combb"][0:16, tt * FT:(tt + 1) * FT], start=False, stop=False)
                            mm = t.matmul(w2v[tt], b["sb2_b"][0:2, oc * 128:(oc + 1) * 128],
                                     b["ones2"][0:2, 0:FT], start=False, stop=True)
                    return mm
                if gi == 0:
                    R.do("tensor", mm_w2, inc="te")
                else:
                    def mm_w2b(t, slot=slot, ei=ei):
                        first = (ei == 0)
                        last = (ei == len(EGROUPS[0]) - 1)
                        for tt in range(NFT):
                            for fc in range(NFC):
                                mm = t.matmul(w2v[tt], w2_ap[slot][:, fc * 128:(fc + 1) * 128],
                                         b["ytil"][:, ei * (NFC * TPC) + fc * TPC + tt * FT: ei * (NFC * TPC) + fc * TPC + (tt + 1) * FT],
                                         start=(first and fc == 0), stop=(last and fc == NFC - 1))
                        return mm
                    R.do("tensor", mm_w2b, inc="te")
                w2_uses.append(R.cnt["te"])
            for tt in range(NFT):
                key = (oc, tt)
                R.wait("sync", "ve")
                R.wait("sync", "st")
                at = b["acct"][:, tt * FT:(tt + 1) * FT]
                R.dma("sync", at, acc_dram[oc][:, tt * FT:(tt + 1) * FT], "la")
                R.wait("vector", "la")
                R.wait("vector", "te")
                R.do("vector", lambda v, at=at, tt=tt: v.tensor_add(out=at, in0=at, in1=w2v[tt]), inc="ve")
                R.wait("sync", "ve")
                R.dma("sync", acc_dram[oc][:, tt * FT:(tt + 1) * FT], at, "st")
                acc_store_cnt[key] = R.cnt["st"]

    trv = [PSA[:, 0:128], PSA[:, 512:640]]
    out_store_st = []
    for j in range(NTC):
        xb = b["xbuf"][:, 0:D]
        R.wait("sync", "ve")
        R.dma("sync", xb, din["x"][j * 128:(j + 1) * 128, :], "lx")
        R.wait("sync", "st")
        for oc in range(NOC):
            R.dma("sync", b["accl"][:, oc * 128:(oc + 1) * 128], acc_dram[oc][:, j * 128:(j + 1) * 128], "la")
        R.wait("tensor", "lx")
        R.wait("tensor", "la")
        if j >= 1:
            R.wait("vector", "st", out_store_st[j - 1])
        for oc in range(NOC):
            src = b["accl"][:, oc * 128:(oc + 1) * 128]
            R.wait("tensor", "ve")
            R.do("tensor", lambda t, d=trv[oc % 2], s=src: t.transpose(d, s, b["identf"][:]), inc="te")
            R.wait("vector", "te")
            R.wait("vector", "lx")
            R.wait("vector", "la")
            dst = b["outb"][:, oc * 128:(oc + 1) * 128]
            xs = xb[:, oc * 128:(oc + 1) * 128]
            R.do("vector", lambda v, d=dst, x=xs, p=trv[oc % 2]: v.tensor_add(out=d, in0=x, in1=p), inc="ve")
        R.wait("sync", "ve")
        R.dma("sync", dout[j * 128:(j + 1) * 128, :], b["outb"][:], "st")
        out_store_st.append(R.cnt["st"])
    R.wait("sync", "st")

    with nc.Block() as bk:
        R.emit(bk)
    es.close()
    return nc


_NC_CACHE = {}


def _get_nc(mode):
    if mode not in _NC_CACHE:
        _NC_CACHE[mode] = build_nc(mode)
    return _NC_CACHE[mode]


def kernel(**inputs):
    L = _host_layouts(inputs)
    x = np.ascontiguousarray(np.asarray(inputs["L0_attn_x"], dtype=np.float32).reshape(S_ALL, D))
    tabs = [_host_tables(c) for c in range(NCORE)]

    base = {"d_" + k: np.ascontiguousarray(np.asarray(v, dtype=np.float32)) for k, v in L.items()}
    in_maps = []
    for c in range(NCORE):
        m = dict(base)
        for nm in ["ccq", "ssq", "cost", "sint", "dhat", "crosst", "kdecs", "gcs", "kdgc", "ident"]:
            m["d_" + nm] = tabs[c][nm]
        m["d_x"] = np.ascontiguousarray(x[c * TPC:(c + 1) * TPC, :])
        in_maps.append(m)

    ncA = _get_nc("summary")
    resA = run_bass_kernel_spmd(ncA, in_maps, core_ids=list(range(NCORE)))
    T_all = np.stack([resA.results[c]["T_out"] for c in range(NCORE)], axis=0)

    ncB = _get_nc("main")
    in_maps2 = []
    for c in range(NCORE):
        m = dict(in_maps[c])
        m["d_T_all"] = T_all
        m["d_wbc"] = np.ascontiguousarray(tabs[c]["wbc"])
        in_maps2.append(m)
    resB = run_bass_kernel_spmd(ncB, in_maps2, core_ids=list(range(NCORE)))
    out = np.concatenate([resB.results[c]["out"] for c in range(NCORE)], axis=0)
    return out.reshape(1, S_ALL, D)
